# version 07 (final)
# baseline (speedup 1.0000x reference)
import numpy as np
import ml_dtypes

import concourse.bass as bass
import concourse.mybir as mybir
from concourse.bass_utils import run_bass_kernel_spmd


F32 = mybir.dt.float32
BF16 = mybir.dt.bfloat16
AF = mybir.ActivationFunctionType
OP = mybir.AluOpType
BF = ml_dtypes.bfloat16

D = 2048
DI = 4096
DS = 128
NH = 64
PD = 64
DXBC = DI + 2 * DS
NXT = DXBC // 128
DCONV = 4
CH = 128
NCHK = 16
TO = 1024
TA = 2048
MT = TA // 128
MO = TO // 128
KD = D // 128
INTER = 8192
EPS_LN = 1e-5
EPS_RMS = 1e-5

DEBUG = False
STOP_AT = 99


def _hadamard(n):
    h = np.array([[1.0]], dtype=np.float32)
    while h.shape[0] < n:
        h = np.block([[h, h], [h, -h]])
    return h


def _host_prepare(inp):
    u = np.asarray(inp["L0_attn_u"], dtype=np.float32)
    W_in = np.asarray(inp["L0_attn_W_in"], dtype=np.float32)
    conv_w = np.asarray(inp["L0_attn_conv_w"], dtype=np.float32)
    conv_b = np.asarray(inp["L0_attn_conv_b"], dtype=np.float32)
    dt_bias = np.asarray(inp["L0_attn_dt_bias"], dtype=np.float32)
    A_log = np.asarray(inp["L0_attn_A_log"], dtype=np.float32)
    Dvec = np.asarray(inp["L0_attn_D"], dtype=np.float32)
    nw = np.asarray(inp["L0_attn_norm_weight"], dtype=np.float32)
    W_out = np.asarray(inp["L0_attn_W_out"], dtype=np.float32)
    SU_ug = np.asarray(inp["L0_ffn_SU_ug"], dtype=np.float32)
    SU_dp = np.asarray(inp["L0_ffn_SU_dp"], dtype=np.float32)
    Wsc_ug = np.asarray(inp["L0_ffn_Wscale_ug"], dtype=np.float32)
    Wsc_dp = np.asarray(inp["L0_ffn_Wscale_dp"], dtype=np.float32)
    W_up = np.asarray(inp["L0_ffn_W_up"], dtype=np.float32)
    W_gate = np.asarray(inp["L0_ffn_W_gate"], dtype=np.float32)
    W_down = np.asarray(inp["L0_ffn_W_down"], dtype=np.float32)

    sh = {}
    wxbc = np.zeros((NXT + 1, KD, 128, 128), dtype=np.float32)
    for e in range(NXT):
        rows = W_in[DI + e * 128: DI + (e + 1) * 128]
        wxbc[e] = rows.reshape(128, KD, 128).transpose(1, 2, 0)
    dt_rows = W_in[DI + DXBC:]
    wxbc[NXT, :, :, :64] = dt_rows.reshape(64, KD, 128).transpose(1, 2, 0)
    sh["wxbc"] = np.ascontiguousarray(wxbc)
    sh["wz"] = np.ascontiguousarray(
        W_in[:DI].reshape(8, 512, KD, 128).transpose(0, 2, 3, 1))
    sh["wout"] = np.ascontiguousarray(
        W_out.T.reshape(32, 128, 4, 512).transpose(2, 0, 1, 3))
    wug = np.zeros((128, KD, 128, 128), dtype=np.float32)
    for j in range(64):
        wug[2 * j] = W_up[j * 128:(j + 1) * 128].reshape(128, KD, 128).transpose(1, 2, 0)
        wug[2 * j + 1] = W_gate[j * 128:(j + 1) * 128].reshape(128, KD, 128).transpose(1, 2, 0)
    sh["wug"] = np.ascontiguousarray(wug)
    sh["wdn"] = np.ascontiguousarray(
        W_down.T.reshape(64, 128, 4, 512).transpose(2, 0, 1, 3))

    sh["convw"] = np.ascontiguousarray(
        conv_w.reshape(NXT, 128, DCONV).transpose(1, 0, 2).reshape(128, NXT * DCONV))
    sh["convb"] = np.ascontiguousarray(conv_b.reshape(NXT, 128).T)
    sh["dtbias"] = np.ascontiguousarray(dt_bias.reshape(64, 1))
    sh["alog"] = np.ascontiguousarray(A_log.reshape(64, 1))
    sh["drep"] = np.ascontiguousarray(np.broadcast_to(Dvec, (128, NH)))
    sh["nwrep"] = np.ascontiguousarray(np.broadcast_to(nw, (128, DI)))
    sh["surep"] = np.ascontiguousarray(np.broadcast_to(SU_ug, (128, D)))
    sh["sudp"] = np.ascontiguousarray(SU_dp.reshape(64, 128).T)
    sh["wscup"] = np.ascontiguousarray(Wsc_ug[:INTER].reshape(64, 128).T)
    sh["wscgt"] = np.ascontiguousarray(Wsc_ug[INTER:].reshape(64, 128).T)
    sh["wscdp"] = np.ascontiguousarray(np.broadcast_to(Wsc_dp, (128, D)))

    sh["h128"] = _hadamard(128).astype(BF)
    sh["ident"] = np.eye(128, dtype=BF)
    sh["uppermask"] = np.triu(np.ones((128, 128), dtype=np.float32)).astype(BF)
    dl = np.zeros((NH, NH, CH), dtype=np.float32)
    for h in range(NH):
        dl[h, h, :] = 1.0
    sh["deltac"] = np.ascontiguousarray(dl.reshape(NH, NH * CH).astype(BF))
    sh["onesrow"] = np.ones((2, 128), dtype=BF)

    maps = []
    for c in range(8):
        s, half = c // 2, c % 2
        m = dict(sh)
        own = u[s, half * 1024:(half + 1) * 1024]
        prev = np.zeros((1024, D), dtype=np.float32) if half == 0 else u[s, 0:1024]
        m["u_own"] = np.ascontiguousarray(own)
        m["u_prev"] = np.ascontiguousarray(prev)
        m["maskcol"] = np.full((128, 1), float(half), dtype=np.float32)
        maps.append(m)
    return maps


class Sems:
    def __init__(self, nc, ctx):
        self.nc, self.ctx, self.d = nc, ctx, {}

    def __call__(self, name):
        if name not in self.d:
            self.d[name] = self.ctx.enter_context(self.nc.semaphore(name))
        return self.d[name]


class Arena:
    def __init__(self, nc, ctx, name, nbytes):
        self.t = ctx.enter_context(nc.sbuf_tensor(name, [128, nbytes], mybir.dt.uint8))
        self.free_list = [(0, nbytes)]
        self.ranges = {}
        self.by_tag = {}

    def alloc(self, n_elems, dtype=BF16, tag=""):
        esz = mybir.dt.size(dtype)
        need = ((n_elems * esz + 31) // 32) * 32
        for i, (s, e) in enumerate(self.free_list):
            if e - s >= need:
                if s + need == e:
                    self.free_list.pop(i)
                else:
                    self.free_list[i] = (s + need, e)
                ap = self.t[:, s:s + n_elems * esz].bitcast(dtype)
                self.ranges[id(ap)] = (s, s + need)
                self.by_tag[tag] = (s, n_elems, dtype)
                return ap
        raise MemoryError(f"arena OOM for {tag}: need {need}B, free={self.free_list}")

    def free(self, *aps):
        for ap in aps:
            self.free_list.append(self.ranges.pop(id(ap)))
        self.free_list.sort()
        merged = []
        for a, b in self.free_list:
            if merged and merged[-1][1] >= a:
                merged[-1] = (merged[-1][0], max(merged[-1][1], b))
            else:
                merged.append((a, b))
        self.free_list = [tuple(x) for x in merged]


def _r(ap, spec, **kw):
    return ap.rearrange(spec, **kw)



PD = 64
NI = 64
EPS = 1e-5
C_UG = 1.0 / np.sqrt(2048.0)
C_DN = 1.0 / np.sqrt(8192.0)

class Sched:

    def __init__(self, sems, tag, names=("t", "v", "s", "y")):
        self.ops = []
        self.sems = {n: sems(tag + "_" + n) for n in names}
        self.cnt = {n: 0 for n in names}

    def op(self, eng, fn, waits=(), inc=None, inc_n=1):
        semobj = None
        if isinstance(inc, str):
            self.cnt[inc] += inc_n
            semobj = self.sems[inc]
        elif inc is not None:
            semobj = inc
        self.ops.append((eng, fn, [w for w in waits if w is not None],
                         semobj, inc_n))
        if isinstance(inc, str):
            return (self.sems[inc], self.cnt[inc])
        return None

    def mark(self, key):
        return (self.sems[key], self.cnt[key])

    def emit(self, block):
        decs = {"t": block.tensor, "v": block.vector, "s": block.scalar,
                "sy": block.sync, "g": block.gpsimd}
        for ek, dec in decs.items():
            myops = [o for o in self.ops if o[0] == ek]
            if not myops:
                continue

            def build(eng, _ops=myops, _ek=ek):
                for _, fn, waits, sem, n in _ops:
                    for (ws, wv) in waits:
                        if wv > 0:
                            eng.wait_ge(ws, wv)
                    inst = fn(eng)
                    if sem is not None:
                        if _ek in ("v", "s"):
                            eng.drain().then_inc(sem, n)
                        else:
                            inst.then_inc(sem, n)
            dec(build)


def emit_ssd(nc, block, sems, AR, bank, env):
    bc_fmv = env["bc_fmv"]
    dt_fm = env["dt_fm"]
    dA_fm = env["dA_fm"]
    x_ownv = env["x_ownv"]
    xprev_scr = env["xprev_scr"]
    y_scr = env["y_scr"]
    deltac = env["deltac"]
    idn = env["idn_sb"]
    upm = env["upm_sb"]
    ones = env["ones_sb"]
    mask = env["mask_sb"]
    drep = env["drep_sb"]
    psum = env["psum"]
    sXfin = sems("xfin")
    sDT = sems("sdt")
    sXsp = [sems("xsp0"), sems("xsp1")]

    RH = AR.alloc(NH * CH, BF16, "RH")
    ET = AR.alloc(NH * CH, BF16, "ET")
    ETv = ET.rearrange("p (h l) -> p h l", h=NH)
    xchv = ET.rearrange("p (c t) -> p c t", c=64)
    state = [AR.alloc(4096, F32, f"st{i}") for i in range(2)]
    statev = [st.rearrange("p (h q) -> p h q", h=NH) for st in state]
    ybuf = [AR.alloc(4096, BF16, f"yb{i}") for i in range(2)]
    x_tm = AR.alloc(4096, BF16, "x_tm")
    x_tmv = x_tm.rearrange("p (h q) -> p h q", h=NH)
    xdt = AR.alloc(4096, BF16, "xdt")
    xdtv = xdt.rearrange("p (h q) -> p h q", h=NH)
    xdtd = AR.alloc(4096, BF16, "xdtd")
    SA = AR.alloc(128, BF16, "SA")
    SB = AR.alloc(128, BF16, "SB")
    Btm = AR.alloc(128, BF16, "Btm")
    CBt = AR.alloc(128, BF16, "CBt")
    dtT = AR.alloc(64, BF16, "dtT")
    decT = AR.alloc(64, BF16, "decT")
    eaT = AR.alloc(64, BF16, "eaT")
    cdrep = AR.alloc(64, F32, "cdrep")
    acs = AR.alloc(128, F32, "acs")
    ahi = AR.alloc(128, BF16, "ahi")
    alo = AR.alloc(128, BF16, "alo")
    dpre = AR.alloc(128, F32, "dpre")
    dtb16 = AR.alloc(128, BF16, "dtb16")
    dec16 = AR.alloc(128, BF16, "dec16")
    ea16 = AR.alloc(128, BF16, "ea16")
    cdc = AR.alloc(1, F32, "cdc")
    cdhl = AR.alloc(2, BF16, "cdhl")
    cdrow = AR.alloc(64, BF16, "cdrow")
    st16 = AR.alloc(4096, BF16, "st16")

    S = Sched(sems, "ssd")
    sRH = [sems(f"rh{i}") for i in range(2)]
    rh_cnt = [0, 0]
    sXL = sems("xl")
    xl_cnt = [0]
    sYS = [sems(f"yst{i}") for i in range(2)]
    ys_cnt = [0, 0]

    S.op("sy", lambda e: e.dma_start(out=RH[0:NH, :], in_=deltac),
         waits=[(sXfin, 34), (sDT, 3)], inc="y", inc_n=16)
    S.op("sy", lambda e: e.dma_start(out=SA[64:66, :], in_=env["onesrow"]),
         waits=[(sXfin, 34), (sDT, 3)], inc="y", inc_n=16)
    delta_mark = S.mark("y")
    S.op("v", lambda e: e.memset(state[0][:], 0.0),
         waits=[(sXfin, 34), (sDT, 3)], inc="v")
    setup_v = S.mark("v")

    diff_mark = [None]
    xT_mark = [None]
    rh_mark = [None]

    for c in range(16):
        own = c >= 8
        sl = slice(c * CH, (c + 1) * CH)

        def v0(e, sl=sl):
            e.tensor_tensor_scan(out=acs[0:64, :], data0=dA_fm[0:64, sl],
                                 data1=dA_fm[0:64, sl], initial=0.0,
                                 op0=OP.add, op1=OP.bypass)
            e.tensor_copy(ahi[0:64, :], acs[0:64, :])
            e.tensor_sub(alo[0:64, :], acs[0:64, :], ahi[0:64, :])
            e.tensor_scalar(out=SA[0:64, :], in0=ahi[0:64, :], scalar1=-1.0,
                            scalar2=None, op0=OP.mult)
            e.tensor_scalar(out=SB[0:64, :], in0=alo[0:64, :], scalar1=-1.0,
                            scalar2=None, op0=OP.mult)
            e.tensor_scalar(out=dpre[0:64, :], in0=acs[0:64, :],
                            scalar1=acs[0:64, 127:128], scalar2=None,
                            op0=OP.subtract)
            e.tensor_copy(dtb16[0:64, :], dt_fm[0:64, sl])
            return e.drain()
        v0m = S.op("v", v0, waits=[setup_v, rh_mark[0]], inc="v")

        def s0(e, own=own):
            e.activation(out=dec16[0:64, :], in_=dpre[0:64, :], func=AF.Exp,
                         scale=-1.0)
            e.activation(out=cdc[0:64, :], in_=acs[0:64, 127:128],
                         func=AF.Exp)
            if own:
                e.activation(out=ea16[0:64, :], in_=acs[0:64, :],
                             func=AF.Exp)
            return e.drain()
        s0m = S.op("s", s0, waits=[v0m], inc="s")

        def v1(e):
            e.tensor_copy(cdhl[0:64, 0:1], cdc[0:64, :])
            e.drain()
            e.tensor_sub(cdhl[0:64, 1:2], cdc[0:64, :], cdhl[0:64, 0:1])
            return e.drain()
        v1m = S.op("v", v1, waits=[s0m], inc="v")

        if own:
            def drh(e, c=c):
                e.dma_start(out=RH[64:65, :],
                            in_=ahi[0:64, :]).then_inc(sRH[c % 2], 16)
                return e.dma_start(out=RH[65:66, :], in_=alo[0:64, :])
            rh_cnt[c % 2] += 32
            S.op("sy", drh, waits=[v0m, delta_mark, diff_mark[0]],
                 inc=sRH[c % 2], inc_n=16)
            rh_mark[0] = (sRH[c % 2], rh_cnt[c % 2])

        if not own:
            def dxl(e, c=c):
                return e.dma_start(
                    out=xchv[:, 0:32, :],
                    in_=xprev_scr[:, :, c * CH:(c + 1) * CH].rearrange(
                        "c p t -> p c t"))
            wx = [xT_mark[0], (sDT, 3)]
            if c == 0:
                wx += [(sXsp[0], 256), (sXsp[1], 256)]
            xl_cnt[0] += 16
            S.op("sy", dxl, waits=wx, inc=sXL, inc_n=16)

        tparts = [(Btm, bc_fmv[:, 0, sl]),
                  (dtT, dtb16[0:64, :]),
                  (decT, dec16[0:64, :])]
        if own:
            tparts.append((eaT, ea16[0:64, :]))
        tparts.append((cdrow, cdhl[0:64, 0:2]))
        ev_marks = []
        for j, (dst, src) in enumerate(tparts):
            b = j % 2
            sp, sf = src.shape[0], src.shape[1]

            def t0(e, src=src, sp=sp, sf=sf, b=b):
                return e.transpose(bank(b, sp, BF16)[0:sf, :], src,
                                   idn[0:sp, 0:sp])
            tw0 = [v0m, s0m, v1m]
            if len(ev_marks) >= 2:
                tw0.append(ev_marks[-2])
            tm = S.op("t", t0, waits=tw0, inc="t")

            def ev(e, dst=dst, sp=sp, sf=sf, b=b):
                return e.tensor_copy(dst[0:sf, 0:sp],
                                     bank(b, sp, BF16)[0:sf, :])
            ev_marks.append(S.op("v", ev, waits=[tm], inc="v"))
        tiny_ev = ev_marks[-1]

        def tcd(e):
            return e.matmul(bank(2, 64), ones[0:2, :], cdrow[0:2, 0:64],
                            start=True, stop=True)
        tcdm = S.op("t", tcd, waits=[tiny_ev], inc="t")

        def vcd(e):
            return e.tensor_copy(cdrep[:, :], bank(2, 64))
        vcdm = S.op("v", vcd, waits=[tcdm], inc="v")

        ev2m = None
        s1m = None
        if own:
            emn = []
            for j in range(16):
                b = j % 2

                def t1(e, j=j, b=b):
                    e.matmul(bank(b), SA[0:66, :],
                             RH[0:66, j * 512:(j + 1) * 512],
                             start=True, stop=False)
                    return e.matmul(bank(b), SB[0:64, :],
                                    RH[0:64, j * 512:(j + 1) * 512],
                                    start=False, stop=True)
                waits = [vcdm, (sRH[c % 2], rh_cnt[c % 2])]
                if len(emn) >= 2:
                    waits.append(emn[-2])
                t1m = S.op("t", t1, waits=waits, inc="t")

                def ev1(e, j=j, b=b):
                    return e.tensor_scalar(out=ET[:, j * 512:(j + 1) * 512],
                                           in0=bank(b), scalar1=0.0,
                                           scalar2=None, op0=OP.min)
                emn.append(S.op("v", ev1, waits=[t1m], inc="v"))
            diff_mark[0] = S.mark("t")

            def s1(e):
                return e.activation(out=ET[:], in_=ET[:], func=AF.Exp)
            s1m = S.op("s", s1, waits=[emn[-1]], inc="s")

            def t2(e, sl=sl):
                return e.matmul(bank(2, 128), bc_fmv[:, 0, sl],
                                bc_fmv[:, 1, sl], start=True, stop=True)
            t2m = S.op("t", t2, waits=[vcdm], inc="t")

            def ev2(e):
                return e.tensor_tensor(out=CBt[:, :], in0=bank(2, 128),
                                       in1=upm[:, :], op=OP.mult)
            ev2m = S.op("v", ev2, waits=[t2m], inc="v")

        xev = []
        for ct in range(32):
            b = ct % 2
            if own:
                src = x_ownv[:, ct, (c - 8) * CH:(c - 7) * CH]
            else:
                src = xchv[:, ct, :]

            def t3(e, src=src, b=b):
                return e.transpose(bank(b, 128, BF16), src, idn[:, :])
            waits = []
            if ct == 0:
                waits = [ev2m if own else vcdm]
                if not own:
                    waits.append((sXL, xl_cnt[0]))
            if len(xev) >= 2:
                waits.append(xev[-2])
            t3m = S.op("t", t3, waits=waits, inc="t")

            def ev3(e, ct=ct, b=b):
                return e.tensor_copy(x_tm[:, ct * 128:(ct + 1) * 128],
                                     bank(b, 128, BF16))
            xev.append(S.op("v", ev3, waits=[t3m], inc="v"))
        xT_mark[0] = S.mark("t")

        def v2(e, own=own):
            if own:
                e.tensor_tensor(out=ETv[:, :, :], in0=ETv[:, :, :],
                                in1=CBt[:].unsqueeze(1).broadcast_to(
                                    [128, NH, CH]), op=OP.mult)
            e.tensor_tensor(out=xdtv[:, :, :], in0=x_tmv[:, :, :],
                            in1=dtT[:].unsqueeze(2).broadcast_to([128, NH, PD]),
                            op=OP.mult)
            return e.tensor_tensor(
                out=xdtd.rearrange("p (h q) -> p h q", h=NH), in0=xdtv[:, :, :],
                in1=decT[:].unsqueeze(2).broadcast_to([128, NH, PD]),
                op=OP.mult)
        w2 = [xev[-1], tiny_ev]
        if own:
            w2 += [s1m, ev2m]
        v2m = S.op("v", v2, waits=w2, inc="v")

        if own:
            vmaskm = None
            if c == 8:
                def vmask(e, c=c):
                    return e.tensor_scalar(out=state[c % 2][:],
                                           in0=state[c % 2][:],
                                           scalar1=mask[:, :], scalar2=None,
                                           op0=OP.mult)
                vmaskm = S.op("v", vmask, waits=[], inc="v")

            def vst(e, c=c):
                return e.tensor_copy(st16[:], state[c % 2][:])
            vstm = S.op("v", vst, waits=[vmaskm] if vmaskm else [], inc="v")

            def t4(e, sl=sl):
                mm = None
                for g in range(8):
                    mm = e.matmul(bank(g), bc_fmv[:, 1, sl],
                                  st16[:, g * 512:(g + 1) * 512],
                                  start=True, stop=True)
                return mm
            t4m = S.op("t", t4, waits=[v2m, vstm], inc="t")

            yb = ybuf[c % 2]
            ybv = yb.rearrange("p (h q) -> p h q", h=NH)

            def v3(e, ybv=ybv):
                return e.tensor_tensor(
                    out=ybv[:, :, :],
                    in0=psum[:, 0:4096].rearrange("p (h q) -> p h q", h=NH),
                    in1=eaT[:].unsqueeze(2).broadcast_to([128, NH, PD]),
                    op=OP.mult)
            w3 = [t4m]
            if ys_cnt[c % 2] > 0:
                w3.append((sYS[c % 2], ys_cnt[c % 2]))
            v3m = S.op("v", v3, waits=w3, inc="v")

            def t5(e):
                mm = None
                for h in range(NH):
                    mm = e.matmul(bank(h // 8)[:, (h % 8) * 64:(h % 8) * 64 + 64],
                                  ETv[:, h, :], xdtv[:, h, :],
                                  start=(h % 8 == 0), stop=(h % 8 == 7))
                return mm
            t5m = S.op("t", t5, waits=[v3m], inc="t")

            def v4(e, yb=yb):
                e.tensor_tensor(out=yb[:], in0=yb[:], in1=psum[:, 0:4096],
                                op=OP.add)
                e.tensor_tensor(out=x_tmv[:, :, :], in0=x_tmv[:, :, :],
                                in1=drep[:].unsqueeze(2).broadcast_to(
                                    [128, NH, PD]), op=OP.mult)
                return e.tensor_tensor(out=yb[:], in0=yb[:], in1=x_tm[:],
                                       op=OP.add)
            v4m = S.op("v", v4, waits=[t5m], inc="v")

            def dys(e, c=c, yb=yb):
                return e.dma_start(out=y_scr[c - 8], in_=yb[:])
            ys_cnt[c % 2] += 16
            S.op("sy", dys, waits=[v4m], inc=sYS[c % 2], inc_n=16)
            tw = [v4m]
        else:
            tw = [v2m]

        def t6(e):
            mm = None
            for g in range(8):
                mm = e.matmul(bank(g), Btm[:, :],
                              xdtd[:, g * 512:(g + 1) * 512],
                              start=True, stop=True)
            return mm
        t6m = S.op("t", t6, waits=tw, inc="t")

        def v5(e, c=c):
            e.tensor_tensor(out=statev[(c + 1) % 2][:, :, :],
                            in0=statev[c % 2][:, :, :],
                            in1=cdrep[:].unsqueeze(2).broadcast_to(
                                [128, NH, PD]), op=OP.mult)
            return e.tensor_tensor(out=state[(c + 1) % 2][:],
                                   in0=state[(c + 1) % 2][:],
                                   in1=psum[:, 0:4096], op=OP.add)
        S.op("v", v5, waits=[t6m, vcdm], inc="v")

    S.emit(block)
    return {"sYS": sYS, "ys_cnt": ys_cnt,
            "final_v": S.mark("v"), "final_t": S.mark("t"),
            "bufs": [RH, ET, state[0], state[1], ybuf[0], ybuf[1], x_tm,
                     xdt, xdtd, SA, SB, Btm, CBt, dtT, decT, eaT, cdrep,
                     acs, ahi, alo, dpre, dtb16, dec16, ea16, cdc, cdhl,
                     cdrow, st16]}


def emit_mix_tail(nc, block, sems, AR, bank, env):
    z_scr = env["z_scr"]
    y_scr = env["y_scr"]
    r1_scr = env["r1_scr"]
    u_own = env["u_own"]
    wout = env["wout"]
    nwrep = env["nwrep"]
    surep = env["surep"]
    idn = env["idn_sb"]
    h128 = env["h128_sb"]
    psum = env["psum"]
    prev = env["prev_waits"]

    S = Sched(sems, "mt")
    sZL = [sems(f"mzl{i}") for i in range(2)]
    sYL = [sems(f"myl{i}") for i in range(2)]
    sUL = [sems(f"mul{i}") for i in range(2)]
    sWO = [sems(f"mwo{i}") for i in range(2)]
    sR1 = [sems(f"mr1{i}") for i in range(2)]

    ygst = AR.alloc(MO * DI, BF16, "ygst")
    ygv = ygst.rearrange("p (i c) -> p i c", i=MO)
    zt = [AR.alloc(DI, BF16, f"zt{i}") for i in range(2)]
    yt = [AR.alloc(DI, BF16, f"yt{i}") for i in range(2)]
    ztmp = AR.alloc(DI, BF16, "ztmp")
    rc = AR.alloc(MO * 8, F32, "rc")
    rcv = rc.rearrange("p (i c) -> p i c", i=MO)
    nw16 = AR.alloc(DI, BF16, "nw16")
    nwf = AR.alloc(DI, F32, "nwf")
    su16 = AR.alloc(D, BF16, "su16")
    suf = AR.alloc(D, F32, "suf")

    S.op("v", lambda e: e.memset(rc[:], 0.0), waits=prev, inc="v")
    S.op("sy", lambda e: e.dma_start(out=nwf[:], in_=nwrep), waits=prev,
         inc="y", inc_n=16)
    S.op("sy", lambda e: e.dma_start(out=suf[:], in_=surep), inc="y", inc_n=16)
    cm = S.mark("y")
    S.op("v", lambda e: e.tensor_copy(nw16[:], nwf[:]), waits=[cm] + prev,
         inc="v")
    S.op("v", lambda e: e.tensor_copy(su16[:], suf[:]), inc="v")
    cast_m = S.mark("v")

    ev_prev = {}
    for i in range(MO):
        def dz(e, i=i):
            return e.dma_start(out=zt[i % 2][:], in_=z_scr[i])

        def dy(e, i=i):
            return e.dma_start(out=yt[i % 2][:], in_=y_scr[i])
        wz = list(prev) if i < 2 else [ev_prev[i - 2]]
        S.op("sy", dz, waits=wz, inc=sZL[i % 2], inc_n=16)
        S.op("sy", dy, waits=wz, inc=sYL[i % 2], inc_n=16)
        ldm = [(sZL[i % 2], 16 * (i // 2 + 1)), (sYL[i % 2], 16 * (i // 2 + 1))]

        def sa(e, i=i):
            return e.activation(out=ztmp[:], in_=zt[i % 2][:], func=AF.Tanh,
                                scale=0.5)
        sam = S.op("s", sa,
                   waits=ldm + [cast_m, ev_prev.get(i - 1)], inc="s")

        def va(e, i=i):
            e.tensor_scalar(out=ztmp[:], in0=ztmp[:], scalar1=0.5, scalar2=0.5,
                            op0=OP.mult, op1=OP.add)
            e.tensor_mul(zt[i % 2][:], zt[i % 2][:], ztmp[:])
            e.tensor_mul(ygv[:, i, :], yt[i % 2][:], zt[i % 2][:])
            e.tensor_mul(ztmp[:], ygv[:, i, :], ygv[:, i, :])
            e.reduce_sum(rcv[:, i, 0:1], ztmp[:], axis=mybir.AxisListType.X)
            return e.drain()
        ev_prev[i] = S.op("v", va, waits=[sam], inc="v")
    passa = S.mark("v")
    if env.get("debug"):
        sDBM = sems("sdbm")

        def ddyg(e):
            for i in range(MO - 1):
                e.dma_start(out=env["dbg_yg"][i],
                            in_=ygv[:, i, :]).then_inc(sDBM, 16)
            return e.dma_start(out=env["dbg_yg"][MO - 1], in_=ygv[:, MO - 1, :])
        S.op("sy", ddyg, waits=[passa], inc=sDBM, inc_n=16)

    def vb(e):
        for i in range(MO):
            e.tensor_scalar(out=rcv[:, i, 1:2], in0=rcv[:, i, 0:1],
                            scalar1=1.0 / DI, scalar2=EPS,
                            op0=OP.mult, op1=OP.add)
        return e.drain()
    vbm = S.op("v", vb, waits=[passa], inc="v")

    def sb_(e):
        for i in range(MO):
            e.activation(out=rcv[:, i, 4:5], in_=rcv[:, i, 1:2], func=AF.Ln)
        e.drain()
        for i in range(MO):
            e.activation(out=rcv[:, i, 5:6], in_=rcv[:, i, 4:5],
                         func=AF.Exp, scale=-0.5)
        return e.drain()
    sbm = S.op("s", sb_, waits=[vbm], inc="s")

    AR.free(zt[0], zt[1], yt[0], yt[1])
    yn = AR.alloc(32 * 1024, BF16, "yn")
    ynv = yn.rearrange("p (c t) -> p c t", c=32)
    ev_last = {}
    tev = None
    for i in range(MO):
        def vc(e, i=i):
            e.tensor_scalar(out=ztmp[:], in0=ygv[:, i, :],
                            scalar1=rcv[:, i, 5:6], scalar2=None, op0=OP.mult)
            return e.tensor_mul(ztmp[:], ztmp[:], nw16[:])
        w = [sbm]
        if tev is not None:
            w.append(tev)
        vcm = S.op("v", vc, waits=w, inc="v")
        for ct in range(32):
            b = ct % 2

            def tc(e, ct=ct, b=b):
                return e.transpose(bank(b, 128, BF16),
                                   ztmp[:, ct * 128:(ct + 1) * 128], idn[:, :])
            tw = [vcm]
            if b in ev_last:
                tw.append(ev_last[b])
            tcm = S.op("t", tc, waits=tw, inc="t")

            def evc(e, i=i, ct=ct, b=b):
                return e.tensor_copy(ynv[:, ct, i * 128:(i + 1) * 128],
                                     bank(b, 128, BF16))
            ev_last[b] = S.op("v", evc, waits=[tcm], inc="v")
        tev = ev_last[31 % 2]
    passc_v = S.mark("v")
    if env.get("debug"):
        sDBY = sems("sdby")

        def ddyn(e):
            for ct in range(31):
                e.dma_start(out=env["dbg_yn"][ct],
                            in_=ynv[:, ct, :]).then_inc(sDBY, 16)
            e.dma_start(out=env["dbg_yn"][31],
                        in_=ynv[:, 31, :]).then_inc(sDBY, 16)
            return e.dma_start(out=env["dbg_rc"], in_=rc[:])
        S.op("sy", ddyn, waits=[passc_v], inc=sDBY, inc_n=16)

    AR.free(ygst)
    om = AR.alloc(MO * D, F32, "om")
    omv = om.rearrange("p (i c) -> p i c", i=MO)
    wob = [AR.alloc(8 * 512, BF16, f"wob{i}") for i in range(2)]
    st2 = AR.alloc(MO * 16, F32, "st2")
    st2v = st2.rearrange("p (i c) -> p i c", i=MO)

    mm_done = {}
    ev_done = {}
    for n in range(4):
        for kb in range(4):
            j = n * 4 + kb

            def dw(e, n=n, kb=kb, j=j):
                return e.dma_start(
                    out=wob[j % 2].rearrange("p (k e) -> p k e", k=8),
                    in_=wout[n, kb * 8:(kb + 1) * 8].rearrange("k d e -> d k e"))
            w = [passc_v] if j < 2 else [mm_done[j - 2]]
            S.op("g", dw, waits=w, inc=sWO[j % 2], inc_n=16)

            def tmm(e, kb=kb, j=j):
                mm = None
                wv = wob[j % 2].rearrange("p (k e) -> p k e", k=8)
                for k in range(8):
                    for m in range(MO):
                        mm = e.matmul(bank(m),
                                      ynv[:, kb * 8 + k, m * 128:(m + 1) * 128],
                                      wv[:, k, :],
                                      start=(kb == 0 and k == 0),
                                      stop=(kb == 3 and k == 7))
                return mm
            w = [(sWO[j % 2], 16 * (j // 2 + 1)), passc_v]
            if kb == 0 and n > 0:
                w.append(ev_done[n - 1])
            mm_done[j] = S.op("t", tmm, waits=w, inc="t")

        def evo(e, n=n):
            last = None
            for m in range(MO):
                last = e.tensor_copy(omv[:, m, n * 512:(n + 1) * 512], bank(m))
            return last
        ev_done[n] = S.op("v", evo, waits=[mm_done[n * 4 + 3]], inc="v")
    wout_v = ev_done[3]
    if env.get("debug"):
        sDBM2 = sems("sdbm2")

        def ddom(e):
            for i in range(MO - 1):
                e.dma_start(out=env["dbg_om"][i],
                            in_=omv[:, i, :]).then_inc(sDBM2, 16)
            return e.dma_start(out=env["dbg_om"][MO - 1], in_=omv[:, MO - 1, :])
        S.op("sy", ddom, waits=[wout_v], inc=sDBM2, inc_n=16)

    AR.free(yn, wob[0], wob[1])
    ut = [AR.alloc(D, F32, f"ut{i}") for i in range(2)]
    r1t = [AR.alloc(D, F32, f"r1t{i}") for i in range(2)]
    x1 = AR.alloc(KD * 1024, BF16, "x1")
    x1v = x1.rearrange("p (k t) -> p k t", k=KD)
    jnk = ztmp[:, 0:D]
    jnk2 = ztmp[:, D:2 * D]

    x1_evb = {}
    x1_ev = {}
    for i in range(MO):
        def du(e, i=i):
            return e.dma_start(out=ut[i % 2][:],
                               in_=u_own[i * 128:(i + 1) * 128, :])
        w = [wout_v] if i < 2 else [x1_ev[i - 2]]
        S.op("sy", du, waits=w, inc=sUL[i % 2], inc_n=16)

        def v_st(e, i=i):
            e.reduce_sum(st2v[:, i, 0:1], omv[:, i, :],
                         axis=mybir.AxisListType.X)
            return e.drain()
        vstm = S.op("v", v_st, waits=[wout_v], inc="v")

        def s_sq(e, i=i):
            return e.activation(out=jnk2, in_=omv[:, i, :], func=AF.Square)
        ssqm = S.op("s", s_sq, waits=[wout_v], inc="s")

        def v_ab(e, i=i):
            c = st2v[:, i, :]
            e.reduce_sum(c[:, 1:2], jnk2, axis=mybir.AxisListType.X)
            e.drain()
            e.tensor_scalar(out=c[:, 2:3], in0=c[:, 0:1], scalar1=c[:, 0:1],
                            scalar2=None, op0=OP.mult)
            e.drain()
            e.tensor_scalar(out=c[:, 2:3], in0=c[:, 2:3], scalar1=-1.0 / D,
                            scalar2=c[:, 1:2], op0=OP.mult, op1=OP.add)
            e.drain()
            e.tensor_scalar(out=c[:, 2:3], in0=c[:, 2:3], scalar1=1.0 / D,
                            scalar2=EPS, op0=OP.mult, op1=OP.add)
            return e.drain()
        vabm = S.op("v", v_ab, waits=[vstm, ssqm], inc="v")

        def s_r(e, i=i):
            c = st2v[:, i, :]
            e.activation(out=c[:, 8:9], in_=c[:, 2:3], func=AF.Ln)
            e.drain()
            e.activation(out=c[:, 9:10], in_=c[:, 8:9], func=AF.Exp,
                         scale=-0.5)
            return e.drain()
        srm = S.op("s", s_r, waits=[vabm], inc="s")

        def v_r1(e, i=i):
            c = st2v[:, i, :]
            e.tensor_scalar(out=c[:, 3:4], in0=c[:, 0:1], scalar1=1.0 / D,
                            scalar2=c[:, 9:10], op0=OP.mult, op1=OP.mult)
            e.drain()
            e.tensor_scalar(out=r1t[i % 2][:], in0=omv[:, i, :],
                            scalar1=c[:, 9:10], scalar2=c[:, 3:4],
                            op0=OP.mult, op1=OP.subtract)
            e.tensor_add(r1t[i % 2][:], r1t[i % 2][:], ut[i % 2][:])
            e.reduce_sum(c[:, 4:5], r1t[i % 2][:], axis=mybir.AxisListType.X)
            return e.drain()
        w = [srm, (sUL[i % 2], 16 * (i // 2 + 1))]
        if i >= 2:
            w.append((sR1[i % 2], 16 * (i // 2)))
        vr1m = S.op("v", v_r1, waits=w, inc="v")

        def dr1(e, i=i):
            return e.dma_start(out=r1_scr[i], in_=r1t[i % 2][:])
        S.op("sy", dr1, waits=[vr1m], inc=sR1[i % 2], inc_n=16)

        def s_sq3(e, i=i):
            return e.activation(out=jnk2, in_=r1t[i % 2][:], func=AF.Square)
        ssq3m = S.op("s", s_sq3, waits=[vr1m], inc="s")

        def v_ab3(e, i=i):
            c = st2v[:, i, :]
            e.reduce_sum(c[:, 5:6], jnk2, axis=mybir.AxisListType.X)
            e.drain()
            e.tensor_scalar(out=c[:, 6:7], in0=c[:, 4:5], scalar1=c[:, 4:5],
                            scalar2=None, op0=OP.mult)
            e.drain()
            e.tensor_scalar(out=c[:, 6:7], in0=c[:, 6:7], scalar1=-1.0 / D,
                            scalar2=c[:, 5:6], op0=OP.mult, op1=OP.add)
            e.drain()
            e.tensor_scalar(out=c[:, 6:7], in0=c[:, 6:7], scalar1=1.0 / D,
                            scalar2=EPS, op0=OP.mult, op1=OP.add)
            return e.drain()
        vab3m = S.op("v", v_ab3, waits=[ssq3m], inc="v")

        def s_r3(e, i=i):
            c = st2v[:, i, :]
            e.activation(out=c[:, 10:11], in_=c[:, 6:7], func=AF.Ln)
            e.drain()
            e.activation(out=c[:, 11:12], in_=c[:, 10:11], func=AF.Exp,
                         scale=-0.5)
            return e.drain()
        sr3m = S.op("s", s_r3, waits=[vab3m], inc="s")

        def v_x1(e, i=i):
            c = st2v[:, i, :]
            e.tensor_scalar(out=c[:, 7:8], in0=c[:, 4:5], scalar1=1.0 / D,
                            scalar2=c[:, 11:12], op0=OP.mult, op1=OP.mult)
            e.drain()
            e.tensor_scalar(out=jnk, in0=r1t[i % 2][:], scalar1=c[:, 11:12],
                            scalar2=c[:, 7:8], op0=OP.mult, op1=OP.subtract)
            return e.tensor_mul(jnk, jnk, su16[:])
        vx1m = S.op("v", v_x1, waits=[sr3m], inc="v")

        for k in range(KD):
            b = k % 2

            def tx(e, k=k, b=b):
                return e.transpose(bank(b, 128, BF16),
                                   jnk[:, k * 128:(k + 1) * 128], idn[:, :])
            tw = [vx1m]
            if b in x1_evb:
                tw.append(x1_evb[b])
            txm = S.op("t", tx, waits=tw, inc="t")

            def evx(e, i=i, k=k, b=b):
                return e.tensor_copy(x1v[:, k, i * 128:(i + 1) * 128],
                                     bank(b, 128, BF16))
            x1_evb[b] = S.op("v", evx, waits=[txm], inc="v")
        x1_ev[i] = x1_evb[(KD - 1) % 2]
    x1_done = S.mark("v")

    AR.free(om)
    x1b = AR.alloc(KD * 1024, BF16, "x1b")
    x1bv = x1b.rearrange("p (k t) -> p k t", k=KD)
    st1_ev = {}
    for k in range(KD):
        for tc in range(2):
            j = 2 * k + tc
            b = j % 4

            def tf(e, k=k, tc=tc, b=b):
                return e.matmul(bank(b), h128[:, :],
                                x1v[:, k, tc * 512:(tc + 1) * 512],
                                start=True, stop=True)
            tw = [x1_done]
            if j >= 4:
                tw.append(st1_ev[j - 4])
            tfm = S.op("t", tf, waits=tw, inc="t")

            def evf(e, k=k, tc=tc, b=b):
                return e.tensor_copy(x1bv[:, k, tc * 512:(tc + 1) * 512],
                                     bank(b))
            st1_ev[j] = S.op("v", evf, waits=[tfm], inc="v")
    st1_done = S.mark("v")

    bufs = [x1bv, x1v]
    cur = 0
    for s_ in range(4):
        step = 1 << s_
        for i0 in range(0, KD, 2 * step):
            for q in range(step):
                a, bI = i0 + q, i0 + q + step

                def vadd(e, a=a, bI=bI, cur=cur):
                    return e.tensor_add(bufs[1 - cur][:, a, :],
                                        bufs[cur][:, a, :], bufs[cur][:, bI, :])

                def vsub(e, a=a, bI=bI, cur=cur):
                    return e.tensor_sub(bufs[1 - cur][:, bI, :],
                                        bufs[cur][:, a, :], bufs[cur][:, bI, :])
                w = [st1_done] if s_ == 0 else []
                S.op("v", vadd, waits=w, inc="v")
                S.op("v", vsub, waits=[], inc="v")
        cur = 1 - cur
    fwht_done = S.mark("v")

    S.emit(block)
    x1h = bufs[cur]
    base = x1b if cur == 0 else x1
    dead = x1 if cur == 0 else x1b
    return {"x1h": x1h, "x1h_base": base, "fwht_done": fwht_done, "sR1": sR1,
            "dead": [dead, ztmp, rc, nw16, nwf, su16, suf, ut[0], ut[1],
                     r1t[0], r1t[1], st2]}


def emit_ffn(nc, block, sems, AR, bank, env):
    x1hv = env["x1h"]
    wug = env["wug"]
    wdn = env["wdn"]
    wscup = env["wscup"]
    wscgt = env["wscgt"]
    sudp = env["sudp"]
    wscdp = env["wscdp"]
    r1_scr = env["r1_scr"]
    out = env["out"]
    psum = env["psum"]
    prev = env["prev_waits"]

    S = Sched(sems, "ff")
    sWU = [sems(f"fwu{i}") for i in range(4)]
    sWD = [sems(f"fwd{i}") for i in range(2)]
    sRL = [sems(f"frl{i}") for i in range(2)]
    sOT = [sems(f"fot{i}") for i in range(2)]

    gg = AR.alloc(NI * 1024, BF16, "gg")
    ggv = gg.rearrange("p (c t) -> p c t", c=NI)
    wub = [AR.alloc(KD * 128, BF16, f"wub{i}") for i in range(4)]
    ups = AR.alloc(1024, BF16, "ups")
    gtmp = AR.alloc(1024, BF16, "gtmp")
    s2t = AR.alloc(1024, BF16, "s2t")
    wcol = AR.alloc(8, F32, "wcol")
    wcf = AR.alloc(128, F32, "wcf")
    wcfv = wcf.rearrange("p (a b) -> p a b", a=2)

    S.op("sy", lambda e: e.dma_start(out=wcfv[:, 0, :], in_=wscup),
         waits=prev, inc="y", inc_n=16)
    S.op("sy", lambda e: e.dma_start(out=wcfv[:, 1, :], in_=wscgt),
         inc="y", inc_n=16)
    sdp = AR.alloc(64, F32, "sdp")
    S.op("sy", lambda e: e.dma_start(out=sdp[:], in_=sudp), inc="y", inc_n=16)
    cm = S.mark("y")
    S.op("v", lambda e: e.tensor_mul(wcfv[:, 0, :], wcfv[:, 0, :], sdp[:]),
         waits=[cm] + prev, inc="v")
    scale_m = S.mark("v")

    if env.get("debug"):
        sDF1 = sems("sdf1")

        def ddx1(e):
            for k in range(KD - 1):
                e.dma_start(out=env["dbg_x1h"][k],
                            in_=x1hv[:, k, :]).then_inc(sDF1, 16)
            return e.dma_start(out=env["dbg_x1h"][KD - 1], in_=x1hv[:, KD - 1, :])
        S.op("sy", ddx1, waits=list(prev), inc=sDF1, inc_n=16)

    dbg_w_m = None
    ev_pair = {}
    mm_pair = {}
    for j in range(NI):
        for half in range(2):
            tile = 2 * j + half
            slot = tile % 4

            def dwu(e, tile=tile, slot=slot):
                return e.dma_start(
                    out=wub[slot].rearrange("p (k e) -> p k e", k=KD),
                    in_=wug[tile].rearrange("k d e -> d k e"))
            w = list(prev) if tile < 4 else [mm_pair[(tile - 4) // 2]]
            if tile in (4, 5) and dbg_w_m is not None:
                w.append(dbg_w_m)
            S.op("g", dwu, waits=w, inc=sWU[slot], inc_n=16)

        bb = (j % 2) * 4

        def tmm(e, j=j, bb=bb):
            mm = None
            for half in range(2):
                slot = (2 * j + half) % 4
                wv = wub[slot].rearrange("p (k e) -> p k e", k=KD)
                for k in range(KD):
                    for tc in range(2):
                        mm = e.matmul(bank(bb + 2 * half + tc),
                                      wv[:, k, :],
                                      x1hv[:, k, tc * 512:(tc + 1) * 512],
                                      start=(k == 0), stop=(k == KD - 1))
            return mm
        w = [(sWU[(2 * j) % 4], 16 * (j // 2 + 1)),
             (sWU[(2 * j + 1) % 4], 16 * (j // 2 + 1)), scale_m]
        if j >= 2:
            w.append(ev_pair[j - 2])
        mm_pair[j] = S.op("t", tmm, waits=w, inc="t")
        if j == 0 and env.get("debug"):
            sDW = sems("sdw")

            def ddw(e):
                e.dma_start(out=env["dbg_wub"][0],
                            in_=wub[0].rearrange("p (k e) -> p k e", k=KD)
                            ).then_inc(sDW, 16)
                return e.dma_start(
                    out=env["dbg_wub"][1],
                    in_=wub[1].rearrange("p (k e) -> p k e", k=KD))
            dbg_w_m = S.op("sy", ddw, waits=[mm_pair[0]], inc=sDW, inc_n=16)
            dbg_w_m = (sDW, 32)

        def vsc(e, j=j, bb=bb):
            for tc in range(2):
                e.tensor_scalar(out=ups[:, tc * 512:(tc + 1) * 512],
                                in0=bank(bb + tc), scalar1=wcfv[:, 0, j:j + 1],
                                scalar2=C_UG, op0=OP.mult, op1=OP.mult)
            last = None
            for tc in range(2):
                last = e.tensor_scalar(out=gtmp[:, tc * 512:(tc + 1) * 512],
                                       in0=bank(bb + 2 + tc),
                                       scalar1=wcfv[:, 1, j:j + 1],
                                       scalar2=C_UG, op0=OP.mult, op1=OP.mult)
            return last
        w = [mm_pair[j]]
        if j >= 1:
            w.append(sil_prev)
        vscm = S.op("v", vsc, waits=w, inc="v")
        stn_w = [vscm]
        if j == 0 and env.get("debug"):
            sDU = sems("sdu")

            def ddu(e):
                e.dma_start(out=env["dbg_ups"][0],
                            in_=ups[:]).then_inc(sDU, 16)
                return e.dma_start(out=env["dbg_ups"][1], in_=gtmp[:])
            S.op("sy", ddu, waits=[vscm], inc=sDU, inc_n=16)
            stn_w.append((sDU, 32))

        def stn(e):
            return e.activation(out=s2t[:], in_=gtmp[:], func=AF.Tanh,
                                scale=0.5)
        stnm = S.op("s", stn, waits=stn_w, inc="s")

        def vga(e, j=j):
            e.tensor_scalar(out=s2t[:], in0=s2t[:], scalar1=0.5, scalar2=0.5,
                            op0=OP.mult, op1=OP.add)
            e.tensor_mul(gtmp[:], gtmp[:], s2t[:])
            return e.tensor_mul(ggv[:, j, :], gtmp[:], ups[:])
        ev_pair[j] = sil_prev = S.op("v", vga, waits=[stnm], inc="v")
    upg_done = S.mark("v")

    x1e_m = None
    if env.get("debug"):
        sDX = sems("sdx")

        def ddx1e(e):
            for k in range(KD - 1):
                e.dma_start(out=env["dbg_x1e"][k],
                            in_=x1hv[:, k, :]).then_inc(sDX, 16)
            return e.dma_start(out=env["dbg_x1e"][KD - 1],
                               in_=x1hv[:, KD - 1, :])
        S.op("sy", ddx1e, waits=[mm_pair[NI - 1]], inc=sDX, inc_n=16)
        x1e_m = (sDX, 16 * KD)

    if env.get("debug"):
        sDF0 = sems("sdf0")

        def ddgg0(e):
            for j in range(NI - 1):
                e.dma_start(out=env["dbg_gg0"][j],
                            in_=ggv[:, j, :]).then_inc(sDF0, 16)
            return e.dma_start(out=env["dbg_gg0"][NI - 1], in_=ggv[:, NI - 1, :])
        S.op("sy", ddgg0, waits=[upg_done], inc=sDF0, inc_n=16)
        upg_done = (sDF0, 16 * NI)

    h128 = env["h128_sb"]
    st1_ev = {}
    for j in range(NI):
        for tc in range(2):
            q = 2 * j + tc
            b = q % 4

            def tf(e, j=j, tc=tc, b=b):
                return e.matmul(bank(b), h128[:, :],
                                ggv[:, j, tc * 512:(tc + 1) * 512],
                                start=True, stop=True)
            tw = [upg_done]
            if q >= 4:
                tw.append(st1_ev[q - 4])
            tfm = S.op("t", tf, waits=tw, inc="t")

            def evf(e, j=j, tc=tc, b=b):
                return e.tensor_copy(ggv[:, j, tc * 512:(tc + 1) * 512],
                                     bank(b))
            st1_ev[q] = S.op("v", evf, waits=[tfm], inc="v")
    st1_done = S.mark("v")
    if env.get("debug"):
        sDF2 = sems("sdf2")

        def ddgg(e):
            for j in range(NI - 1):
                e.dma_start(out=env["dbg_gg"][j],
                            in_=ggv[:, j, :]).then_inc(sDF2, 16)
            return e.dma_start(out=env["dbg_gg"][NI - 1], in_=ggv[:, NI - 1, :])
        S.op("sy", ddgg, waits=[st1_done], inc=sDF2, inc_n=16)
        st1_done = (sDF2, 16 * NI)

    sp = [AR.alloc(1024, BF16, f"fsp{i}") for i in range(2)]

    def slotv(p):
        if isinstance(p, int):
            return ggv[:, p, :]
        return p[:]
    phys = list(range(NI))
    spare = [sp[0], sp[1]]
    first = True
    for s_ in range(6):
        step = 1 << s_
        for i0 in range(0, NI, 2 * step):
            for q in range(step):
                a, bI = i0 + q, i0 + q + step
                pa, pb = phys[a], phys[bI]
                na, nb = spare[0], spare[1]

                def vadd(e, pa=pa, pb=pb, na=na):
                    return e.tensor_add(slotv(na), slotv(pa), slotv(pb))

                def vsub(e, pa=pa, pb=pb, nb=nb):
                    return e.tensor_sub(slotv(nb), slotv(pa), slotv(pb))
                w = [st1_done] if first else []
                first = False
                S.op("v", vadd, waits=w, inc="v")
                S.op("v", vsub, waits=[], inc="v")
                spare = [pa, pb]
                phys[a], phys[bI] = na, nb
    st2_done = S.mark("v")

    if env.get("x1h_base") is not None:
        AR.free(env["x1h_base"])
    AR.free(wub[0], wub[1], wub[2], wub[3], ups, gtmp, s2t, wcol, wcf, sdp)
    of = AR.alloc(MO * D, BF16, "of")
    wdb = [AR.alloc(8 * 512, BF16, f"wdb{i}") for i in range(2)]
    ofv = of.rearrange("p (i c) -> p i c", i=MO)
    wdpf = AR.alloc(D, F32, "wdpf")
    wdp16 = AR.alloc(D, BF16, "wdp16")
    S.op("sy", lambda e: e.dma_start(out=wdpf[:], in_=wscdp),
         waits=prev, inc="y", inc_n=16)
    wm = S.mark("y")
    S.op("v", lambda e: e.tensor_scalar(out=wdp16[:], in0=wdpf[:],
                                        scalar1=C_DN, scalar2=None,
                                        op0=OP.mult),
         waits=[wm], inc="v")
    wdp_m = S.mark("v")

    mm_done = {}
    ev_done = {}
    for n in range(4):
        for kb in range(8):
            j = n * 8 + kb

            def dwd(e, n=n, kb=kb, j=j):
                return e.dma_start(
                    out=wdb[j % 2].rearrange("p (k e) -> p k e", k=8),
                    in_=wdn[n, kb * 8:(kb + 1) * 8].rearrange("k d e -> d k e"))
            w = [st2_done] if j < 2 else [mm_done[j - 2]]
            if j < 2 and x1e_m is not None:
                w.append(x1e_m)
            S.op("g", dwd, waits=w, inc=sWD[j % 2], inc_n=16)

            def tdm(e, kb=kb, j=j, phys_snap=tuple(phys)):
                mm = None
                wv = wdb[j % 2].rearrange("p (k e) -> p k e", k=8)
                for k in range(8):
                    src = slotv(phys_snap[kb * 8 + k])
                    for m in range(MO):
                        mm = e.matmul(bank(m),
                                      src[:, m * 128:(m + 1) * 128],
                                      wv[:, k, :],
                                      start=(kb == 0 and k == 0),
                                      stop=(kb == 7 and k == 7))
                return mm
            w = [(sWD[j % 2], 16 * (j // 2 + 1)), st2_done, wdp_m]
            if kb == 0 and n > 0:
                w.append(ev_done[n - 1])
            mm_done[j] = S.op("t", tdm, waits=w, inc="t")

        def evd(e, n=n):
            last = None
            for m in range(MO):
                last = e.tensor_tensor(out=ofv[:, m, n * 512:(n + 1) * 512],
                                       in0=bank(m),
                                       in1=wdp16[:, n * 512:(n + 1) * 512],
                                       op=OP.mult)
            return last
        ew = [mm_done[n * 8 + 7]]
        if n == 0 and x1e_m is not None:
            ew.append(x1e_m)
        ev_done[n] = S.op("v", evd, waits=ew, inc="v")
    wdn_v = ev_done[3]

    if env.get("debug"):
        sDF3 = sems("sdf3")

        def ddof(e):
            for i in range(MO - 1):
                e.dma_start(out=env["dbg_of"][i],
                            in_=ofv[:, i, :]).then_inc(sDF3, 16)
            return e.dma_start(out=env["dbg_of"][MO - 1], in_=ofv[:, MO - 1, :])
        S.op("sy", ddof, waits=[wdn_v], inc=sDF3, inc_n=16)

    AR.free(gg, sp[0], sp[1], wdb[0], wdb[1], wdpf)
    st4 = AR.alloc(MO * 8, F32, "st4")
    st4v = st4.rearrange("p (i c) -> p i c", i=MO)
    r1t = [AR.alloc(D, F32, f"fr1{i}") for i in range(2)]
    ot = [AR.alloc(D, F32, f"fot{i}") for i in range(2)]
    jnk = AR.alloc(D, BF16, "fjnk")
    jnk2 = AR.alloc(D, BF16, "fjnk2")

    ot_prev = {}
    for i in range(MO):
        def dr(e, i=i):
            return e.dma_start(out=r1t[i % 2][:], in_=r1_scr[i])
        w = [wdn_v] if i < 2 else [ot_prev[i - 2]]
        S.op("sy", dr, waits=w, inc=sRL[i % 2], inc_n=16)

        def v_st(e, i=i):
            e.reduce_sum(st4v[:, i, 0:1], ofv[:, i, :],
                         axis=mybir.AxisListType.X)
            return e.drain()
        vstm = S.op("v", v_st, waits=[wdn_v], inc="v")

        def s_sq(e, i=i):
            return e.activation(out=jnk2, in_=ofv[:, i, :], func=AF.Square)
        ssqm = S.op("s", s_sq, waits=[wdn_v], inc="s")

        def v_ab(e, i=i):
            c = st4v[:, i, :]
            e.reduce_sum(c[:, 1:2], jnk2, axis=mybir.AxisListType.X)
            e.drain()
            e.tensor_scalar(out=c[:, 2:3], in0=c[:, 0:1], scalar1=c[:, 0:1],
                            scalar2=None, op0=OP.mult)
            e.drain()
            e.tensor_scalar(out=c[:, 2:3], in0=c[:, 2:3], scalar1=-1.0 / D,
                            scalar2=c[:, 1:2], op0=OP.mult, op1=OP.add)
            e.drain()
            e.tensor_scalar(out=c[:, 2:3], in0=c[:, 2:3], scalar1=1.0 / D,
                            scalar2=EPS, op0=OP.mult, op1=OP.add)
            return e.drain()
        vabm = S.op("v", v_ab, waits=[vstm, ssqm], inc="v")

        def s_r(e, i=i):
            c = st4v[:, i, :]
            e.activation(out=c[:, 4:5], in_=c[:, 2:3], func=AF.Ln)
            e.drain()
            e.activation(out=c[:, 5:6], in_=c[:, 4:5], func=AF.Exp,
                         scale=-0.5)
            return e.drain()
        srm = S.op("s", s_r, waits=[vabm], inc="s")

        def v_out(e, i=i):
            c = st4v[:, i, :]
            e.tensor_scalar(out=c[:, 3:4], in0=c[:, 0:1], scalar1=1.0 / D,
                            scalar2=c[:, 5:6], op0=OP.mult, op1=OP.mult)
            e.drain()
            e.tensor_scalar(out=ot[i % 2][:], in0=ofv[:, i, :],
                            scalar1=c[:, 5:6], scalar2=c[:, 3:4],
                            op0=OP.mult, op1=OP.subtract)
            return e.tensor_add(ot[i % 2][:], ot[i % 2][:], r1t[i % 2][:])
        w = [srm, (sRL[i % 2], 16 * (i // 2 + 1))]
        if i >= 2:
            w.append((sOT[i % 2], 16 * (i // 2)))
        vom = S.op("v", v_out, waits=w, inc="v")

        def dout(e, i=i):
            return e.dma_start(out=out[i * 128:(i + 1) * 128, :],
                               in_=ot[i % 2][:])
        ot_prev[i] = S.op("sy", dout, waits=[vom], inc=sOT[i % 2], inc_n=16)

    S.emit(block)
    final_waits = [(sOT[0], 16 * 4), (sOT[1], 16 * 4),
                   (S.sems["y"], 64),
                   (sWU[0], 16 * 32), (sWU[1], 16 * 32),
                   (sWU[2], 16 * 32), (sWU[3], 16 * 32),
                   (sWD[0], 16 * 16), (sWD[1], 16 * 16),
                   (sRL[0], 16 * 4), (sRL[1], 16 * 4)]
    return {"final_waits": final_waits}




def build_program(debug=DEBUG, stop_at=STOP_AT):
    from contextlib import ExitStack
    nc = bass.Bass("TRN2", target_bir_lowering=False,
                   detect_race_conditions=False)

    def inp(name, shape, dt=F32):
        return nc.dram_tensor(name, list(shape), dt, kind="ExternalInput").ap()

    u_own = inp("u_own", (TO, D))
    u_prev = inp("u_prev", (TO, D))
    wxbc = inp("wxbc", (NXT + 1, KD, 128, 128))
    wz = inp("wz", (8, KD, 128, 512))
    wout = inp("wout", (4, 32, 128, 512))
    wug = inp("wug", (128, KD, 128, 128))
    wdn = inp("wdn", (4, 64, 128, 512))
    convw = inp("convw", (128, NXT * DCONV))
    convb = inp("convb", (128, NXT))
    dtbias = inp("dtbias", (64, 1))
    alog = inp("alog", (64, 1))
    drep = inp("drep", (128, NH))
    nwrep = inp("nwrep", (128, DI))
    surep = inp("surep", (128, D))
    sudp = inp("sudp", (128, 64))
    wscup = inp("wscup", (128, 64))
    wscgt = inp("wscgt", (128, 64))
    wscdp = inp("wscdp", (128, D))
    maskcol = inp("maskcol", (128, 1))
    h128 = inp("h128", (128, 128), BF16)
    ident = inp("ident", (128, 128), BF16)
    uppermask = inp("uppermask", (128, 128), BF16)
    deltac = inp("deltac", (NH, NH * CH), BF16)
    onesrow = inp("onesrow", (2, 128), BF16)

    out = nc.dram_tensor("out", [TO, D], F32, kind="ExternalOutput").ap()

    scr_kind = "ExternalOutput" if debug else "Internal"
    z_scr = nc.dram_tensor("z_scr", [MO, 128, DI], BF16, kind=scr_kind).ap()
    xprev_scr = nc.dram_tensor("xprev_scr", [32, 128, TO], BF16, kind=scr_kind).ap()
    r1_scr = nc.dram_tensor("r1_scr", [MO, 128, D], F32, kind=scr_kind).ap()
    y_scr = nc.dram_tensor("y_scr", [MO, 128, DI], BF16, kind=scr_kind).ap()
    if debug:
        dbg_xown = nc.dram_tensor("dbg_xown", [32, 128, TO], BF16, kind="ExternalOutput").ap()
        dbg_bc = nc.dram_tensor("dbg_bc", [2, 128, TA], BF16, kind="ExternalOutput").ap()
        dbg_dt = nc.dram_tensor("dbg_dt", [64, TA], F32, kind="ExternalOutput").ap()
        dbg_yg = nc.dram_tensor("dbg_yg", [MO, 128, DI], BF16, kind="ExternalOutput").ap()
        dbg_om = nc.dram_tensor("dbg_om", [MO, 128, D], F32, kind="ExternalOutput").ap()
        dbg_uln = nc.dram_tensor("dbg_uln", [KD, 128, TA], BF16, kind="ExternalOutput").ap()
        dbg_yn = nc.dram_tensor("dbg_yn", [32, 128, TO], BF16, kind="ExternalOutput").ap()
        dbg_rc = nc.dram_tensor("dbg_rc", [128, MO * 8], F32, kind="ExternalOutput").ap()
        dbg_x1h = nc.dram_tensor("dbg_x1h", [KD, 128, TO], BF16, kind="ExternalOutput").ap()
        dbg_gg = nc.dram_tensor("dbg_gg", [64, 128, TO], BF16, kind="ExternalOutput").ap()
        dbg_gg0 = nc.dram_tensor("dbg_gg0", [64, 128, TO], BF16, kind="ExternalOutput").ap()
        dbg_wub = nc.dram_tensor("dbg_wub", [2, 128, KD, 128], BF16, kind="ExternalOutput").ap()
        dbg_ups = nc.dram_tensor("dbg_ups", [2, 128, TO], BF16, kind="ExternalOutput").ap()
        dbg_x1e = nc.dram_tensor("dbg_x1e", [KD, 128, TO], BF16, kind="ExternalOutput").ap()
        dbg_of = nc.dram_tensor("dbg_of", [MO, 128, D], BF16, kind="ExternalOutput").ap()
        dbg_st = nc.dram_tensor("dbg_st", [128, MT * 8], F32, kind="ExternalOutput").ap()

    ctx = ExitStack()
    with ctx:
        sems = Sems(nc, ctx)
        AR = Arena(nc, ctx, "arena", 207 * 1024)
        psum = ctx.enter_context(nc.psum_tensor("ps", [128, 8 * 512], F32))

        def bank(b, w=512, dt=F32):
            if dt == F32:
                return psum[:, b * 512:b * 512 + w]
            return psum[:, b * 512:(b + 1) * 512].bitcast(dt)[:, :w]

        block = ctx.enter_context(nc.Block())

        cw_sb = AR.alloc(NXT * DCONV, F32, "cw")
        cw_v = _r(cw_sb, "p (c j) -> p c j", c=NXT)
        cb_sb = AR.alloc(NXT, F32, "cb")
        idn_sb = AR.alloc(128, BF16, "idn")
        h128_sb = AR.alloc(128, BF16, "h128")
        upm_sb = AR.alloc(128, BF16, "upm")
        ones_sb = AR.alloc(128, BF16, "ones")
        mask_sb = AR.alloc(1, F32, "mask")
        dtb_sb = AR.alloc(1, F32, "dtb")
        alog_sb = AR.alloc(1, F32, "alogsb")
        acol_sb = AR.alloc(1, F32, "acol")
        drep_sb = AR.alloc(NH, BF16, "drepsb")
        drep_f32 = AR.alloc(NH, F32, "drepf")
        u_ln = AR.alloc(KD * TA, BF16, "u_ln")
        u_lnv = _r(u_ln, "p (k t) -> p k t", k=KD)
        bc_fm = AR.alloc(2 * TA, BF16, "bc_fm")
        bc_fmv = _r(bc_fm, "p (c t) -> p c t", c=2)
        dt_fm = AR.alloc(TA, F32, "dt_fm")
        stats = AR.alloc(MT * 8, F32, "stats")
        statsv = _r(stats, "p (i c) -> p i c", i=MT)

        sLC = sems("lc")
        NLC = 10

        @block.sync
        def _(sy):
            sy.dma_start(out=cw_sb[:], in_=convw).then_inc(sLC, 16)
            sy.dma_start(out=cb_sb[:], in_=convb).then_inc(sLC, 16)
            sy.dma_start(out=idn_sb[:], in_=ident).then_inc(sLC, 16)
            sy.dma_start(out=h128_sb[:], in_=h128).then_inc(sLC, 16)
            sy.dma_start(out=upm_sb[:], in_=uppermask).then_inc(sLC, 16)
            sy.dma_start(out=ones_sb[0:2, :], in_=onesrow).then_inc(sLC, 16)
            sy.dma_start(out=mask_sb[:], in_=maskcol).then_inc(sLC, 16)
            sy.dma_start(out=dtb_sb[0:64, :], in_=dtbias).then_inc(sLC, 16)
            sy.dma_start(out=alog_sb[0:64, :], in_=alog).then_inc(sLC, 16)
            sy.dma_start(out=drep_f32[:], in_=drep).then_inc(sLC, 16)

        @block.scalar
        def _(s):
            s.wait_ge(sLC, 16 * NLC)
            s.activation(out=acol_sb[0:64, :], in_=alog_sb[0:64, :], func=AF.Exp)
            s.drain()

        @block.vector
        def _(v):
            v.memset(stats[:], 0.0)
            v.wait_ge(sLC, 16 * NLC)
            v.tensor_copy(drep_sb[:], drep_f32[:])

        sUr = [sems(f"sU{i}") for i in range(4)]
        sP0a = sems("p0a")
        sP0w = sems("p0w")
        sP0s = sems("p0s")
        sP0t = sems("p0t")
        sP0e = sems("p0e")
        sP0x = sems("p0x")

        utile = [AR.alloc(D, F32, f"utile{i}") for i in range(4)]
        lntile = [AR.alloc(D, BF16, f"lntile{i}") for i in range(2)]
        vjunk = AR.alloc(D, BF16, "vjunk")
        sjunk = AR.alloc(D, BF16, "sjunk")

        @block.sync
        def _(sy):
            for i in range(MT):
                src = (u_prev if i < 8 else u_own)
                row = (i % 8) * 128
                if i >= 4:
                    sy.wait_ge(sP0a, i - 3)
                    sy.wait_ge(sP0s, 2 * (i - 4) + 1)
                sy.dma_start(out=utile[i % 4][:],
                             in_=src[row:row + 128, :]).then_inc(sUr[i % 4], 16)

        @block.vector
        def _(v):
            for i in range(MT):
                ut = utile[i % 4]
                c = statsv[:, i, :]
                v.wait_ge(sUr[i % 4], 16 * (i // 4 + 1))
                v.reduce_sum(c[:, 0:1], ut[:],
                             axis=mybir.AxisListType.X)
                v.nop().then_inc(sP0x, 1)
                v.wait_ge(sP0s, 2 * i + 1)
                v.reduce_sum(c[:, 1:2], sjunk[:], axis=mybir.AxisListType.X)
                v.drain()
                v.tensor_scalar(out=c[:, 2:3], in0=c[:, 0:1], scalar1=c[:, 0:1],
                                scalar2=None, op0=OP.mult)
                v.drain()
                v.tensor_scalar(out=c[:, 2:3], in0=c[:, 2:3], scalar1=-1.0 / D,
                                scalar2=c[:, 1:2], op0=OP.mult, op1=OP.add)
                v.drain()
                v.tensor_scalar(out=c[:, 2:3], in0=c[:, 2:3], scalar1=1.0 / D,
                                scalar2=EPS_LN, op0=OP.mult, op1=OP.add)
                v.drain()
                v.nop().then_inc(sP0w, 1)
                v.wait_ge(sP0s, 2 * i + 2)
                v.tensor_scalar(out=c[:, 3:4], in0=c[:, 0:1], scalar1=1.0 / D,
                                scalar2=c[:, 5:6], op0=OP.mult, op1=OP.mult)
                v.drain()
                lt = lntile[i % 2]
                if i >= 2:
                    v.wait_ge(sP0t, 16 * (i - 1))
                v.tensor_scalar(out=lt[:], in0=ut[:], scalar1=c[:, 5:6],
                                scalar2=c[:, 3:4], op0=OP.mult,
                                op1=OP.subtract)
                v.drain().then_inc(sP0a, 1)
                if i >= 1:
                    for k in range(KD):
                        v.wait_ge(sP0t, 16 * (i - 1) + k + 1)
                        v.tensor_copy(u_lnv[:, k, (i - 1) * 128:i * 128],
                                      bank(k % 2, 128, BF16))
                        v.drain().then_inc(sP0e, 1)
            for k in range(KD):
                v.wait_ge(sP0t, 16 * (MT - 1) + k + 1)
                v.tensor_copy(u_lnv[:, k, (MT - 1) * 128:MT * 128],
                              bank(k % 2, 128, BF16))
                v.drain().then_inc(sP0e, 1)

        @block.scalar
        def _(s):
            for i in range(MT):
                ut = utile[i % 4]
                c = statsv[:, i, :]
                s.wait_ge(sUr[i % 4], 16 * (i // 4 + 1))
                s.activation(out=sjunk[:], in_=ut[:], func=AF.Square)
                s.drain().then_inc(sP0s, 1)
                s.wait_ge(sP0w, i + 1)
                s.activation(out=c[:, 4:5], in_=c[:, 2:3], func=AF.Ln)
                s.drain()
                s.activation(out=c[:, 5:6], in_=c[:, 4:5], func=AF.Exp,
                             scale=-0.5)
                s.drain()
                s.nop().then_inc(sP0s, 1)

        @block.tensor
        def _(t):
            for i in range(MT):
                t.wait_ge(sP0a, i + 1)
                lt = lntile[i % 2]
                for k in range(KD):
                    j = 16 * i + k
                    if j >= 2:
                        t.wait_ge(sP0e, j - 1)
                    t.transpose(bank(k % 2, 128, BF16),
                                lt[:, k * 128:(k + 1) * 128],
                                idn_sb[:]).then_inc(sP0t, 1)

        P0A_END, P0E_END = MT, 16 * MT
        AR.free(utile[0], utile[1], utile[2], utile[3], lntile[0], lntile[1])

        if stop_at <= 1:
            _finish_partial(nc, block, sems, [(sems("p0e"), P0E_END)])
            return nc

        if debug:
            sDBU = sems("sdbu")

            @block.sync
            def _(sy):
                sy.wait_ge(sP0e, P0E_END)
                for k in range(KD):
                    sy.dma_start(out=dbg_uln[k], in_=u_lnv[:, k, :]).then_inc(sDBU, 16)
                sy.dma_start(out=dbg_st, in_=stats[:]).then_inc(sDBU, 16)
                sy.wait_ge(sDBU, 16 * (KD + 1))

        sWZ = [sems(f"wzs{i}") for i in range(2)]
        sZmm = sems("zmm")
        sZev = sems("zev")
        sZst = [sems(f"zst{i}") for i in range(2)]
        wzb = [AR.alloc(KD * 512, BF16, f"wzb{i}") for i in range(2)]
        zob = [AR.alloc(8 * 512, BF16, f"zob{i}") for i in range(2)]

        @block.gpsimd
        def _(g):
            g.wait_ge(sP0e, P0E_END)
            for n in range(8):
                if n >= 2:
                    g.wait_ge(sZmm, 2 * (n - 1))
                g.dma_start(out=_r(wzb[n % 2], "p (k n) -> p k n", k=KD),
                            in_=_r(wz[n], "k d e -> d k e")).then_inc(sWZ[n % 2], 16)

        @block.tensor
        def _(t):
            t.wait_ge(sP0e, P0E_END)
            for n in range(8):
                t.wait_ge(sWZ[n % 2], 16 * (n // 2 + 1))
                w = _r(wzb[n % 2], "p (k n) -> p k n", k=KD)
                for g_ in range(2):
                    if n >= 1:
                        t.wait_ge(sZev, 2 * (n - 1) + g_ + 1)
                    mm = None
                    for k in range(KD):
                        for m in range(4 * g_, 4 * g_ + 4):
                            mm = t.matmul(bank(m), u_lnv[:, k, (8 + m) * 128:(9 + m) * 128],
                                          w[:, k, :], start=(k == 0),
                                          stop=(k == KD - 1))
                    mm.then_inc(sZmm, 1)

        @block.vector
        def _(v):
            for n in range(8):
                o = _r(zob[n % 2], "p (m n) -> p m n", m=8)
                for g_ in range(2):
                    v.wait_ge(sZmm, 2 * n + g_ + 1)
                    if n >= 2 and g_ == 0:
                        v.wait_ge(sZst[n % 2], 16 * (n // 2))
                    cp = None
                    for m in range(4 * g_, 4 * g_ + 4):
                        cp = v.tensor_copy(o[:, m, :], bank(m))
                    v.drain().then_inc(sZev, 1)

        @block.sync
        def _(sy):
            for n in range(8):
                sy.wait_ge(sZev, 2 * n + 2)
                sy.dma_start(out=_r(z_scr[:, :, n * 512:(n + 1) * 512],
                                    "m p n -> p m n"),
                             in_=_r(zob[n % 2], "p (m n) -> p m n", m=8),
                             ).then_inc(sZst[n % 2], 16)

        if stop_at <= 2:
            _finish_partial(nc, block, sems,
                            [(sZst[0], 16 * 4), (sZst[1], 16 * 4)])
            return nc

        AR.free(wzb[0], wzb[1], zob[0], zob[1])

        x_own = AR.alloc(32 * TO, BF16, "x_own")
        x_ownv = _r(x_own, "p (c t) -> p c t", c=32)
        dA_fm = AR.alloc(TA, F32, "dA_fm")
        NE = NXT + 1
        sWX = [sems(f"wxs{i}") for i in range(2)]
        sXmm = sems("xmm")
        sXev = sems("xev")
        sXrd = sems("xrd")
        sXbi = sems("xbi")
        sXfin = sems("xfin")
        sXcs = sems("xcs")
        sXsp = [sems(f"xsp{i}") for i in range(2)]
        sDT = sems("sdt")
        wxb = [AR.alloc(KD * 128, BF16, f"wxb{i}") for i in range(2)]
        raw = [AR.alloc(TA + 6, BF16, f"raw{i}") for i in range(2)]
        ctmp = [AR.alloc(TA, BF16, f"ctmp{i}") for i in range(3)]
        stmp = AR.alloc(TA, BF16, "stmp")
        spill = [AR.alloc(TO, BF16, f"spill{i}") for i in range(2)]

        @block.vector
        def _(v):
            v.wait_ge(sZst[0], 64)
            v.wait_ge(sZst[1], 64)
            v.memset(raw[0][:, 0:3], 0.0)
            v.memset(raw[1][:, 0:3], 0.0).then_inc(sXev, 1)

        @block.gpsimd
        def _(g):
            g.wait_ge(sZmm, 16)
            for e in range(NE):
                if e >= 2:
                    g.wait_ge(sXmm, 4 * (e - 1))
                g.dma_start(out=_r(wxb[e % 2], "p (k e) -> p k e", k=KD),
                            in_=_r(wxbc[e], "k d e -> d k e")).then_inc(sWX[e % 2], 16)

        @block.tensor
        def _(t):
            t.wait_ge(sZev, 16)
            for e in range(NE):
                t.wait_ge(sWX[e % 2], 16 * (e // 2 + 1))
                w = _r(wxb[e % 2], "p (k e) -> p k e", k=KD)
                for tc in range(4):
                    b = (e % 2) * 4 + tc
                    if e >= 2:
                        t.wait_ge(sXev, 1 + 4 * (e - 2) + tc + 1)
                    mm = None
                    for k in range(KD):
                        mm = t.matmul(bank(b), w[:, k, :],
                                      u_lnv[:, k, tc * 512:(tc + 1) * 512],
                                      start=(k == 0), stop=(k == KD - 1))
                    mm.then_inc(sXmm, 1)

        @block.vector
        def _(v):
            for e in range(NE - 1):
                r = raw[e % 2]
                for tc in range(4):
                    b = (e % 2) * 4 + tc
                    v.wait_ge(sXmm, 4 * e + tc + 1)
                    if e >= 2 and tc == 0:
                        v.wait_ge(sXrd, e - 1)
                    v.tensor_copy(r[:, 3 + tc * 512: 3 + (tc + 1) * 512],
                                  bank(b))
                    v.nop().then_inc(sXev, 1)
                c0, c1, c2 = ctmp[0], ctmp[1], ctmp[2]
                if e >= 1:
                    v.wait_ge(sXcs, e)
                v.tensor_scalar(out=c0[:], in0=r[:, 0:TA], scalar1=cw_v[:, e, 0:1],
                                scalar2=None, op0=OP.mult)
                v.tensor_scalar(out=c2[:], in0=r[:, 1:TA + 1],
                                scalar1=cw_v[:, e, 1:2], scalar2=None,
                                op0=OP.mult)
                v.tensor_add(c1[:], c2[:], c0[:])
                v.tensor_scalar(out=c2[:], in0=r[:, 2:TA + 2],
                                scalar1=cw_v[:, e, 2:3], scalar2=None,
                                op0=OP.mult)
                v.tensor_add(c0[:], c2[:], c1[:])
                v.tensor_scalar(out=c2[:], in0=r[:, 3:TA + 3],
                                scalar1=cw_v[:, e, 3:4], scalar2=None,
                                op0=OP.mult).then_inc(sXrd, 1)
                v.tensor_add(c1[:], c2[:], c0[:])
                v.tensor_scalar(out=c1[:], in0=c1[:], scalar1=cb_sb[:, e:e + 1],
                                scalar2=None, op0=OP.add)
                v.drain().then_inc(sXbi, 1)
                v.wait_ge(sXcs, e + 1)
                v.tensor_scalar(out=stmp[:], in0=stmp[:], scalar1=0.5, scalar2=0.5,
                                op0=OP.mult, op1=OP.add)
                if e < 32:
                    if e >= 2:
                        v.wait_ge(sXsp[e % 2], 16 * (e // 2))
                    v.tensor_mul(spill[e % 2][:], c1[:, 0:TO], stmp[:, 0:TO])
                    v.tensor_mul(x_ownv[:, e, :], c1[:, TO:TA],
                                 stmp[:, TO:TA])
                    v.drain().then_inc(sXfin, 1)
                else:
                    v.tensor_mul(bc_fmv[:, e - 32, :], c1[:],
                                 stmp[:])
                    v.drain().then_inc(sXfin, 1)
            for tc in range(4):
                b = ((NE - 1) % 2) * 4 + tc
                v.wait_ge(sXmm, 4 * (NE - 1) + tc + 1)
                v.tensor_copy(dt_fm[0:64, tc * 512:(tc + 1) * 512],
                              bank(b)[0:64, :])
                v.tensor_copy(vjunk[0:1, 0:1], bank(b)[0:1, 0:1]).then_inc(sXev, 1)
            v.drain().then_inc(sDT, 1)

        @block.scalar
        def _(s):
            for e in range(NE - 1):
                s.wait_ge(sXbi, e + 1)
                if e >= 1:
                    s.wait_ge(sXfin, e)
                s.activation(out=stmp[:], in_=ctmp[1][:], func=AF.Tanh,
                             scale=0.5)
                s.drain().then_inc(sXcs, 1)
            s.wait_ge(sDT, 1)
            s.activation(out=dt_fm[0:64, :], in_=dt_fm[0:64, :], func=AF.Exp,
                         bias=dtb_sb[0:64, :])
            s.drain()
            s.activation(out=dt_fm[0:64, :], in_=dt_fm[0:64, :], func=AF.Ln,
                         bias=1.0)
            s.drain()
            s.nop().then_inc(sDT, 1)

        @block.vector
        def _(v):
            v.wait_ge(sDT, 2)
            v.tensor_scalar(out=dA_fm[0:64, :], in0=dt_fm[0:64, :],
                            scalar1=acol_sb[0:64, :], scalar2=-1.0,
                            op0=OP.mult, op1=OP.mult)
            v.drain().then_inc(sDT, 1)

        @block.sync
        def _(sy):
            for e in range(32):
                sy.wait_ge(sXfin, e + 1)
                sy.dma_start(out=xprev_scr[e], in_=spill[e % 2][:],
                             ).then_inc(sXsp[e % 2], 16)

        if debug:
            sDB = sems("sdb")

            @block.sync
            def _(sy):
                sy.wait_ge(sXfin, 34)
                sy.wait_ge(sDT, 3)
                for e in range(32):
                    sy.dma_start(out=dbg_xown[e], in_=x_ownv[:, e, :]).then_inc(sDB, 16)
                sy.dma_start(out=dbg_bc[0], in_=bc_fmv[:, 0, :]).then_inc(sDB, 16)
                sy.dma_start(out=dbg_bc[1], in_=bc_fmv[:, 1, :]).then_inc(sDB, 16)
                sy.dma_start(out=dbg_dt, in_=dt_fm[0:64, :]).then_inc(sDB, 16)

        if stop_at <= 3:
            waits = [(sems("xfin"), 34), (sems("sdt"), 3),
                     (sems("zst0"), 64), (sems("zst1"), 64)]
            waits += [(sems("xsp0"), 16 * 16), (sems("xsp1"), 16 * 16)]
            if debug:
                waits.append((sems("sdb"), 16 * 35))
            _finish_partial(nc, block, sems, waits)
            return nc

        AR.free(u_ln, wxb[0], wxb[1], raw[0], raw[1], ctmp[0], ctmp[1],
                ctmp[2], stmp, spill[0], spill[1], vjunk, sjunk)
        env = dict(bc_fmv=bc_fmv, dt_fm=dt_fm, dA_fm=dA_fm, x_ownv=x_ownv,
                   xprev_scr=xprev_scr, y_scr=y_scr, deltac=deltac,
                   idn_sb=idn_sb, upm_sb=upm_sb, ones_sb=ones_sb,
                   mask_sb=mask_sb, drep_sb=drep_sb, psum=psum,
                   onesrow=onesrow)
        ssd = emit_ssd(nc, block, sems, AR, bank, env)
        globals()["LAST_ARENA"] = AR

        if stop_at <= 5:
            waits = [(sems("yst0"), 16 * 4), (sems("yst1"), 16 * 4),
                     (sems("xl"), 16 * 8), (sems("ssd_y"), 32),
                     (sems("xsp0"), 256), (sems("xsp1"), 256),
                     (sems("zst0"), 64), (sems("zst1"), 64)]
            if debug:
                waits.append((sems("sdb"), 16 * 35))
            _finish_partial(nc, block, sems, waits)
            return nc

        AR.free(*ssd["bufs"])
        AR.free(x_own, bc_fm, dt_fm, dA_fm, stats)
        prev_waits = [(sems("yst0"), 16 * 4), (sems("yst1"), 16 * 4),
                      ssd["final_v"], ssd["final_t"],
                      (sems("zst0"), 64), (sems("zst1"), 64)]
        env2 = dict(z_scr=z_scr, y_scr=y_scr, r1_scr=r1_scr, u_own=u_own,
                    wout=wout, nwrep=nwrep, surep=surep, idn_sb=idn_sb,
                    h128_sb=h128_sb, psum=psum, prev_waits=prev_waits,
                    debug=debug,
                    dbg_yg=dbg_yg if debug else None,
                    dbg_om=dbg_om if debug else None,
                    dbg_yn=dbg_yn if debug else None,
                    dbg_rc=dbg_rc if debug else None)
        mt = emit_mix_tail(nc, block, sems, AR, bank, env2)

        if stop_at <= 7:
            waits = [(sems("mr10"), 16 * 4), (sems("mr11"), 16 * 4),
                     (sems("xl"), 16 * 8), (sems("ssd_y"), 32),
                     (sems("xsp0"), 256), (sems("xsp1"), 256),
                     (sems("mt_y"), 32), (sems("mzl0"), 64),
                     (sems("mzl1"), 64), (sems("myl0"), 64),
                     (sems("myl1"), 64), (sems("mul0"), 64),
                     (sems("mul1"), 64), (sems("mwo0"), 128),
                     (sems("mwo1"), 128),
                     mt["fwht_done"]]
            if debug:
                waits += [(sems("sdb"), 16 * 35), (sems("sdbm"), 128),
                          (sems("sdbm2"), 128)]
            _finish_partial(nc, block, sems, waits)
            return nc

        AR.free(*mt["dead"])
        env3 = dict(x1h=mt["x1h"], x1h_base=mt["x1h_base"], wug=wug, wdn=wdn,
                    wscup=wscup, wscgt=wscgt, sudp=sudp, wscdp=wscdp,
                    r1_scr=r1_scr, out=out, psum=psum, h128_sb=h128_sb,
                    prev_waits=[mt["fwht_done"],
                                (mt["sR1"][0], 16 * 4),
                                (mt["sR1"][1], 16 * 4)],
                    debug=debug,
                    dbg_x1h=dbg_x1h if debug else None,
                    dbg_gg=dbg_gg if debug else None,
                    dbg_gg0=dbg_gg0 if debug else None,
                    dbg_wub=dbg_wub if debug else None,
                    dbg_ups=dbg_ups if debug else None,
                    dbg_x1e=dbg_x1e if debug else None,
                    dbg_of=dbg_of if debug else None)
        ff = emit_ffn(nc, block, sems, AR, bank, env3)

        waits = [(sems("mr10"), 16 * 4), (sems("mr11"), 16 * 4),
                 (sems("xl"), 16 * 8), (sems("ssd_y"), 32),
                 (sems("xsp0"), 256), (sems("xsp1"), 256),
                 (sems("mt_y"), 32), (sems("mzl0"), 64),
                 (sems("mzl1"), 64), (sems("myl0"), 64),
                 (sems("myl1"), 64), (sems("mul0"), 64),
                 (sems("mul1"), 64), (sems("mwo0"), 128),
                 (sems("mwo1"), 128),
                 (sems("yst0"), 16 * 4), (sems("yst1"), 16 * 4),
                 (sems("zst0"), 64), (sems("zst1"), 64)]
        waits += ff["final_waits"]
        if debug:
            waits += [(sems("sdb"), 16 * 35), (sems("sdbm"), 128),
                      (sems("sdbm2"), 128), (sems("sdby"), 16 * 33),
                      (sems("sdbu"), 16 * 17), (sems("sdf1"), 16 * KD),
                      (sems("sdf2"), 16 * NI), (sems("sdf3"), 16 * MO)]
        _finish_partial(nc, block, sems, waits)
    return nc


def _finish_partial(nc, block, sems, waits):
    @block.sync
    def _(sy):
        for sem, val in waits:
            sy.wait_ge(sem, val)


_CACHE = {}


def kernel(**inputs):
    maps = _host_prepare(inputs)
    if "nc" not in _CACHE:
        _CACHE["nc"] = build_program()
    nc = _CACHE["nc"]
    res = run_bass_kernel_spmd(nc, maps, core_ids=list(range(8)))
    outs = [np.asarray(r["out"], dtype=np.float32) for r in res.results]
    return np.stack(outs).reshape(4, 2048, 2048)
